# version 20
# baseline (speedup 1.0000x reference)
import contextlib
import numpy as np
import ml_dtypes
import concourse.bass as bass
import concourse.mybir as mybir
from concourse.bass_utils import run_bass_kernel_spmd

F32 = mybir.dt.float32
BF16 = mybir.dt.bfloat16
AF = mybir.ActivationFunctionType
ALU = mybir.AluOpType
AX = mybir.AxisListType
f32 = np.float32

P = 128
D = 1024
DC = 8
NEXP = 16
FF = 1024
EPS = 1e-5


class Sem:
    def __init__(self, ctx, nc, name):
        self.h = ctx.enter_context(nc.semaphore(name))
        self.v = 0

    def inc(self, instr, n=1):
        instr.then_inc(self.h, n)
        self.v += n
        return instr

    def wait(self, eng):
        eng.wait_ge(self.h, self.v)

    def wait_at(self, eng, v):
        eng.wait_ge(self.h, v)


def build_nc(T=4096, routed=tuple(range(NEXP)), debug=False):
    NT = T // P
    NSPAN = T // 512
    HALF = T // 2
    NTH = NT // 2
    SPH = HALF // 512
    assert T % 1024 == 0
    experts = [("r", e) for e in routed] + [("s", 0), ("s", 1)]
    jobs = [(h, kind, e) for h in range(2) for (kind, e) in experts]

    nc = bass.Bass("TRN2", target_bir_lowering=False)

    x = nc.dram_tensor("x", [T, D], F32, kind="ExternalInput").ap()
    Wq = nc.dram_tensor("Wq", [D, D], F32, kind="ExternalInput").ap()
    Wk = nc.dram_tensor("Wk", [D, D], F32, kind="ExternalInput").ap()
    Wv = nc.dram_tensor("Wv", [D, D], F32, kind="ExternalInput").ap()
    Wo = nc.dram_tensor("Wo", [D, D], F32, kind="ExternalInput").ap()
    rw = nc.dram_tensor("rw", [NEXP, D], F32, kind="ExternalInput").ap()
    rbias = nc.dram_tensor("rbias", [1, NEXP], F32, kind="ExternalInput").ap()
    gatew = nc.dram_tensor("gatew", [NEXP, D, FF], F32, kind="ExternalInput").ap()
    upw = nc.dram_tensor("upw", [NEXP, D, FF], F32, kind="ExternalInput").ap()
    downw = nc.dram_tensor("downw", [NEXP, FF, D], F32, kind="ExternalInput").ap()
    shg = nc.dram_tensor("shg", [D, 2 * FF], F32, kind="ExternalInput").ap()
    shu = nc.dram_tensor("shu", [D, 2 * FF], F32, kind="ExternalInput").ap()
    shd = nc.dram_tensor("shd", [2 * FF, D], F32, kind="ExternalInput").ap()

    ident = nc.inline_tensor(np.eye(P, dtype=ml_dtypes.bfloat16), name="ident").ap()
    ones1 = nc.inline_tensor(np.ones((1, P), dtype=np.float32), name="ones1").ap()

    out = nc.dram_tensor("out", [T, D], F32, kind="ExternalOutput").ap()

    dbg = {}
    if debug:
        for nm, shp in [("xnT", [P, DC * T]), ("scores", [P, NT * 16]),
                        ("combine", [P, NT * 16]), ("attnO", [T, D])]:
            dbg[nm] = nc.dram_tensor("dbg_" + nm, shp, F32, kind="ExternalOutput").ap()

    ctx = contextlib.ExitStack()
    with ctx:
        HEAP = 206 * 1024
        heap = ctx.enter_context(nc.sbuf_tensor("heap", [P, HEAP], mybir.dt.uint8))

        def cf(off, n):
            assert off % 64 == 0 and off + 4 * n <= HEAP, (off, n)
            return heap[:, off:off + 4 * n].bitcast(F32)

        def cb(off, n):
            assert off % 64 == 0 and off + 2 * n <= HEAP, (off, n)
            return heap[:, off:off + 2 * n].bitcast(BF16)

        def align(v):
            return (v + 63) & ~63

        o = 0
        xnT = cb(o, DC * T); o += 2 * DC * T
        scores = cf(o, NT * 16); o += 4 * NT * 16
        combine = cf(o, NT * 16); o += 4 * NT * 16
        st = cf(o, 6 * NT); o += align(4 * 6 * NT)
        ST_MS, ST_SQ, ST_M, ST_R, ST_A, ST_B = (i * NT for i in range(6))
        smx = cf(o, 32); o += 128
        biasbc = cf(o, 16); o += 64
        rwT = cb(o, 128); o += 256
        idn = cb(o, 128); o += 256
        rbias_sb = cf(o, 16); o += 64
        ones_sb = cf(o, 128); o += 512
        adiag = cb(o, 8 * 128); o += 2 * 8 * 128
        rwst = cb(o, 1024); o += 2048
        o = align(o)
        LONG_END = o

        o = LONG_END
        xbuf = cf(o, 2 * D); o += 8 * D
        xnbf = cb(o, 2 * D); o += 4 * D
        bst = cf(o, 16); o += 64
        escore = cf(o, 16); o += 64
        P12_END = o
        assert P12_END <= LONG_END + 24 * 1024
        o = LONG_END
        p3_spb = cf(o, NT * 16); o += 4 * NT * 16
        p3_msk = cf(o, NT * 16); o += 4 * NT * 16
        p3_r16 = cf(o, NT * 16); o += 4 * NT * 16
        p3_tmp = cf(o, NT * 16); o += 4 * NT * 16
        p3_em = cf(o, NT * 16); o += 4 * NT * 16
        p3_tw = cf(o, NT * 16); o += 4 * NT * 16
        p3_gs = cf(o, NT * 4); o += align(4 * NT * 4)
        p3_rg = cf(o, NT * 4); o += align(4 * NT * 4)
        p3_gm = cf(o, NT * 4); o += align(4 * NT * 4)
        assert o <= LONG_END + 16 * 1024 + 4096, o

        P4O = LONG_END + 24 * 1024
        wst = cb(P4O, DC * D)
        wT1 = cb(P4O + 16 * 1024, DC * D)
        wT2 = cb(P4O + 32 * 1024, DC * D)
        wT3 = cb(P4O + 48 * 1024, DC * D)
        wT4 = cb(P4O + 64 * 1024, DC * D)
        qt = cb(P4O + 80 * 1024, D)
        kt = cb(P4O + 82 * 1024, D)
        P4_END = P4O + 84 * 1024
        esc = cf(P4O + 32 * 1024, 1024)
        vspan = cb(P4O, 8 * 512)
        aspan = cb(P4O + 8 * 1024, 8 * 512)
        otile = cf(P4O + 16 * 1024, 2 * D)
        xeb = cf(P4O + 24 * 1024, 2 * D)
        lgst = cf(HEAP - 2048, NT * 16)
        assert NT * 16 * 4 <= 2048

        o = LONG_END
        gwb = cb(o, DC * FF); o += 2 * DC * FF
        uwb = cb(o, DC * FF); o += 2 * DC * FF
        dwb = cb(o, DC * D); o += 2 * DC * D
        gfb = cb(o, 512); o += 2 * 512
        gfb2 = cb(o, 512); o += 2 * 512
        gpf = cb(o, 8 * 512); o += 2 * 8 * 512
        gpf2 = cb(o, 8 * 512); o += 2 * 8 * 512
        o = align(o)
        acc = cf(o, NTH * D); o += 4 * NTH * D
        P5_END = o

        assert max(P12_END, P4_END, P5_END) <= HEAP - 2048, (P12_END, P4_END, P5_END, HEAP)

        pheap = ctx.enter_context(nc.psum_tensor("pheap", [P, 8 * 512], F32))

        def bankf(b, n=512, col=0):
            return pheap[:, b * 512 + col: b * 512 + col + n]

        def bankb(b, n, col=0):
            return pheap[:, b * 512:(b + 1) * 512].bitcast(BF16)[:, col:col + n]

        def zap(hs, hp):
            c = hp * 128 + hs * 64
            return pheap[:, 4 * 512 + c: 4 * 512 + c + 64][hs * 64:(hs + 1) * 64, :]

        def eap(hs, hp):
            c = hp * 128 + hs * 64
            return esc[hs * 64:(hs + 1) * 64, c:c + 64]

        sx0 = Sem(ctx, nc, "sx0"); sx1 = Sem(ctx, nc, "sx1")
        sw = Sem(ctx, nc, "sw")
        sv = Sem(ctx, nc, "sv")
        ssc = Sem(ctx, nc, "ssc")
        stt = Sem(ctx, nc, "stt")
        sev = Sem(ctx, nc, "sev")
        ssig = Sem(ctx, nc, "ssig")
        sao0 = Sem(ctx, nc, "sao0"); sao1 = Sem(ctx, nc, "sao1")
        scst = Sem(ctx, nc, "scst")
        swg = Sem(ctx, nc, "swg"); swu = Sem(ctx, nc, "swu"); swd = Sem(ctx, nc, "swd")
        swgA = Sem(ctx, nc, "swgA"); swuA = Sem(ctx, nc, "swuA")
        ssi = Sem(ctx, nc, "ssi"); sgp = Sem(ctx, nc, "sgp"); sde = Sem(ctx, nc, "sde")
        sxe0 = Sem(ctx, nc, "sxe0"); sxe1 = Sem(ctx, nc, "sxe1")
        s2t = Sem(ctx, nc, "s2t"); s2e = Sem(ctx, nc, "s2e")
        swt = Sem(ctx, nc, "swt")
        srt = Sem(ctx, nc, "srt")
        sxt0 = Sem(ctx, nc, "sxt0"); sxt1 = Sem(ctx, nc, "sxt1")
        saccs = [Sem(ctx, nc, "sacc%d" % k) for k in range(max(1, 4))]

        with nc.Block() as block:
            def p0_sync(s):
                scst.inc(s.dma_start(idn, ident), 16)
                scst.inc(s.dma_start(biasbc[:, :], rbias.broadcast_to((P, NEXP))), 16)
            block.sync(p0_sync)

            def p0_gp(g):
                sw.inc(g.dma_start(rwst[0:16, :], rw), 16)
                sw.inc(g.dma_start(
                    wst[:, :].rearrange("p (c f) -> p c f", f=D),
                    Wq.rearrange("(c p) f -> p c f", p=P)), 16)
                sw.inc(g.dma_start(
                    wT3[:, :].rearrange("p (c f) -> p c f", f=D),
                    Wk.rearrange("(c p) f -> p c f", p=P)), 16)
            block.gpsimd(p0_gp)

        def emit_wtrans(block, wdst, sw_need):
            base = sev.v
            for rc in range(8):
                for dcc in range(8):
                    n = rc * 8 + dcc

                    def tb(t, rc=rc, dcc=dcc, n=n, base=base, sw_need=sw_need):
                        if n == 0:
                            sw.wait_at(t, sw_need)
                            scst.wait(t)
                        if sev.v - 1 >= base:
                            sev.wait_at(t, sev.v - 1)
                        elif base > 0:
                            sev.wait_at(t, base)
                        stt.inc(t.transpose(bankb(n % 2, 128),
                                            wst[:, rc * D + dcc * 128: rc * D + (dcc + 1) * 128],
                                            idn), 1)
                    block.tensor(tb)

                    def sb(sc, rc=rc, dcc=dcc, n=n):
                        stt.wait(sc)
                        sev.inc(sc.copy(wdst[:, dcc * D + rc * 128: dcc * D + (rc + 1) * 128],
                                        bankb(n % 2, 128)), 1)
                    block.scalar(sb)

        def wtile_T(wdst, rc):
            return wdst[:, :].rearrange("p (c t) -> p c t", t=D)[:, :, rc * 128:(rc + 1) * 128]

        var_at = [0] * NT; rstd_at = [0] * NT
        ts_at = [0] * NT; sxe_at = [0] * NT; score_at = [0] * NT; rt_at = [0] * NT

        with nc.Block() as block:
            def p12_ms(v):
                v.memset(adiag[:, :], 0.0)
                sv.inc(v.memset(pheap[:, 4 * 512:6 * 512], 0.0), 1)
            block.vector(p12_ms)
            Z_MS = sv.v
            WO_SWT = [0]

            def p12_wq_t(s):
                sw.wait_at(s, 48)
                scst.wait(s)
                swt.inc(s.dma_start(
                    rwT[:, :].rearrange("p (c j) -> p c j", j=16),
                    rwst[0:16, :], transpose=True), 16)
                for rc in range(8):
                    swt.inc(s.dma_start(wtile_T(wT1, rc), wst[:, rc * D:(rc + 1) * D],
                                        transpose=True), 16)
                for rc in range(8):
                    swt.inc(s.dma_start(wtile_T(wT2, rc), wT3[:, rc * D:(rc + 1) * D],
                                        transpose=True), 16)
            block.sync(p12_wq_t)
            WQ_SWT = swt.v - 128
            WK_SWT_CONST = WQ_SWT + 128

            LAG = min(8, NT - 1)

            def emit_qk(k):
                sxtp_k = sxt0 if k % 2 == 0 else sxt1

                def p4b_t(t, k=k, sxtp_k=sxtp_k):
                    sxtp_k.wait_at(t, rt_at[k])
                    if k == 0:
                        sv.wait_at(t, Z_MS)
                        swt.wait_at(t, WK_SWT_CONST)
                    else:
                        sev.wait(t)
                        for hp in range(8):
                            b = 4 + hp // 4
                            colc = (hp % 4) * 128
                            t.matmul(bankf(b, 128, colc), kt[:, hp * 128:(hp + 1) * 128],
                                     qt[:, hp * 128:(hp + 1) * 128],
                                     start=False, stop=False, skip_group_check=True)
                    mm = None
                    for c in range(8):
                        lhs = xnT[:, c * T + k * P: c * T + (k + 1) * P]
                        for wT, b0, b1 in ((wT1, 0, 1), (wT2, 2, 6)):
                            mm = t.matmul(bankf(b0, 512), lhs,
                                          wT[:, c * D: c * D + 512],
                                          start=(c == 0), stop=(c == 7))
                            mm = t.matmul(bankf(b1, 512), lhs,
                                          wT[:, c * D + 512: c * D + 1024],
                                          start=(c == 0), stop=(c == 7))
                    stt.inc(mm, 1)
                block.tensor(p4b_t)

                def p4b_v(v, k=k):
                    stt.wait(v)
                    v.tensor_copy(qt[:, 0:512], bankf(0, 512))
                    v.tensor_copy(qt[:, 512:1024], bankf(1, 512))
                    v.tensor_copy(kt[:, 0:512], bankf(2, 512))
                    sev.inc(v.tensor_copy(kt[:, 512:1024], bankf(6, 512)), 1)
                block.vector(p4b_v)


            def emit_router(i):
                rb = 3 if i % 2 == 0 else 7
                sxtp_i = sxt0 if i % 2 == 0 else sxt1

                def p12_mm(t, i=i, rb=rb, sxtp_i=sxtp_i):
                    sxtp_i.wait_at(t, rt_at[i])
                    if i == 0:
                        swt.wait_at(t, WK_SWT_CONST)
                    if i >= 2:
                        ssig.wait_at(t, ssig.v - 1)
                    mm = None
                    for c in range(8):
                        mm = t.matmul(bankf(rb, 16), xnT[:, c * T + i * P: c * T + (i + 1) * P],
                                      rwT[:, c * 16:(c + 1) * 16],
                                      start=(c == 0), stop=(c == 7))
                    stt.inc(mm, 1)
                block.tensor(p12_mm)

                def p12_lc(v, i=i, rb=rb):
                    stt.wait(v)
                    ssig.inc(v.tensor_copy(lgst[:, 16 * i:16 * i + 16], bankf(rb, 16)), 1)
                block.vector(p12_lc)


            for i in range(NT):
                par = i % 2
                sxp = sx0 if par == 0 else sx1
                sxtp = sxt0 if par == 0 else sxt1

                if i >= LAG:
                    emit_qk(i - LAG)
                if NT >= 16 and i > NT - 8:
                    _q = NT // 8
                    for rr in range(_q * (i - (NT - 8) - 1), _q * (i - (NT - 8) - 1) + _q):
                        emit_router(rr)

                def p12_ld(s, i=i, par=par, sxp=sxp):
                    if i >= 2:
                        sv.wait_at(s, ts_at[i - 2])
                    sxp.inc(s.dma_start(xbuf[:, par * D:(par + 1) * D],
                                        x[i * P:(i + 1) * P, :]), 16)
                block.sync(p12_ld)

                def p12_v1(v, i=i, par=par, sxp=sxp):
                    sxp.wait(v)
                    if i > 0:
                        v.drain()
                    xb = xbuf[:, par * D:(par + 1) * D]
                    v.bn_stats(bst[:, 0:6], xb[:, 0:512])
                    v.bn_stats(bst[:, 6:12], xb[:, 512:1024])
                    v.drain()
                    sv.inc(v.bn_aggr(st[:, 2 * i:2 * i + 2], bst[:, 0:12]), 1)
                block.vector(p12_v1)
                var_at[i] = sv.v

                def p12_s2(sc, i=i):
                    sv.wait_at(sc, var_at[i])
                    sc.activation(st[:, ST_B + i:ST_B + i + 1], st[:, 2 * i + 1:2 * i + 2],
                                  AF.Ln, bias=f32(EPS), scale=f32(1.0))
                    sc.drain()
                    ssc.inc(sc.activation(st[:, ST_R + i:ST_R + i + 1], st[:, ST_B + i:ST_B + i + 1],
                                          AF.Exp, bias=f32(0.0), scale=f32(-0.5)), 1)
                block.scalar(p12_s2)
                rstd_at[i] = ssc.v

                def p12_v2(v, i=i, par=par, sxtp=sxtp):
                    ssc.wait_at(v, rstd_at[i])
                    if i >= 2:
                        sxtp.wait(v)
                    sv.inc(v.tensor_scalar(
                        xnbf[:, par * D:(par + 1) * D], xbuf[:, par * D:(par + 1) * D],
                        st[:, 2 * i:2 * i + 1], st[:, ST_R + i:ST_R + i + 1],
                        ALU.subtract, ALU.mult), 1)
                block.vector(p12_v2)
                ts_at[i] = sv.v

                def p12_xt(s, i=i, par=par, sxtp=sxtp):
                    sv.wait_at(s, ts_at[i])
                    sxtp.inc(s.dma_start(
                        xnT[:, :].rearrange("p (c t) -> p c t", t=T)[:, :, i * 128:(i + 1) * 128],
                        xnbf[:, par * D:(par + 1) * D], transpose=True), 16)
                block.sync(p12_xt)
                rt_at[i] = sxtp.v

                if i == NT // 2:
                    def p4b_wv_t(s):
                        sw.wait_at(s, 64)
                        for rc in range(8):
                            swt.inc(s.dma_start(wtile_T(wT3, rc), wst[:, rc * D:(rc + 1) * D],
                                                transpose=True), 16)
                    block.sync(p4b_wv_t)
                    WV_SWT = swt.v

                    def p4b_stage_wo(g):
                        swt.wait_at(g, WV_SWT)
                        sw.wait_at(g, 64)
                        sw.inc(g.dma_start(
                            wst[:, :].rearrange("p (c f) -> p c f", f=D),
                            Wo.rearrange("(c p) f -> p c f", p=P)), 16)
                    block.gpsimd(p4b_stage_wo)
                    WO_SW = sw.v

                if i == 3 * NT // 4:
                    def p4b_wo_t(s):
                        sw.wait_at(s, WO_SW)
                        for rc in range(8):
                            swt.inc(s.dma_start(wtile_T(wT4, rc), wst[:, rc * D:(rc + 1) * D],
                                                transpose=True), 16)
                    block.sync(p4b_wo_t)
                    WO_SWT[0] = swt.v

                if i == max(1, NT // 8):
                    def p12_stage_wv(g):
                        swt.wait_at(g, WK_SWT_CONST)
                        sw.wait_at(g, 48)
                        sw.inc(g.dma_start(
                            wst[:, :].rearrange("p (c f) -> p c f", f=D),
                            Wv.rearrange("(c p) f -> p c f", p=P)), 16)
                    block.gpsimd(p12_stage_wv)

            for k in range(NT - LAG, NT):
                emit_qk(k)

            if NT < 16:
                for i in range(NT):
                    emit_router(i)
            else:
                for i in range(NT - NT // 8, NT):
                    emit_router(i)

            def p12_sexp(sc):
                ssig.wait(sc)
                ssc.inc(sc.activation(p3_tmp[:, 0:NT * 16], lgst[:, 0:NT * 16],
                                      AF.Exp, bias=f32(0.0), scale=f32(-1.0)), 1)
            block.scalar(p12_sexp)

            def p12_srec(v):
                ssc.wait(v)
                v.tensor_scalar(p3_r16[:, :], p3_tmp[:, 0:NT * 16], 1.0, None, ALU.add)
                v.drain()
                sv.inc(v.reciprocal(scores[:, :], p3_r16[:, :]), 1)
            block.vector(p12_srec)
            SCORES_BATCH = sv.v

            def p4b_fin(t):
                sev.wait(t)
                mm = None
                for hp in range(8):
                    b = 4 + hp // 4
                    colc = (hp % 4) * 128
                    mm = t.matmul(bankf(b, 128, colc), kt[:, hp * 128:(hp + 1) * 128],
                                  qt[:, hp * 128:(hp + 1) * 128],
                                  start=False, stop=False, skip_group_check=True)
                stt.inc(mm, 1)
            block.tensor(p4b_fin)

        SCORES_DONE = SCORES_BATCH

        def build_routing_steps():
            s3 = lambda ap: ap.rearrange("p (t e) -> p t e", e=16)
            s4 = lambda ap: ap.rearrange("p (t g j) -> p t g j", g=4, j=4)
            g3 = lambda ap: ap.rearrange("p (t g) -> p t g", g=4)
            steps = []

            def step0(v):
                sv.wait_at(v, SCORES_DONE)
                scst.wait(v)
                for tix in range(NT):
                    v.tensor_tensor(p3_spb[:, 16 * tix:16 * tix + 16],
                                    scores[:, 16 * tix:16 * tix + 16], biasbc[:, :], ALU.add)
                v.drain()
            steps.append(step0)

            pairs = [(a, b) for a in range(4) for b in range(a + 1, 4)]

            def mkpair(n, a, b):
                def f(v):
                    dst = p3_gs if n == 0 else p3_rg
                    v.tensor_tensor(g3(dst), s4(p3_spb)[:, :, :, a], s4(p3_spb)[:, :, :, b], ALU.add)
                    v.drain()
                    if n > 0:
                        v.tensor_tensor(p3_gs[:, :], p3_gs[:, :], p3_rg[:, :], ALU.max)
                        v.drain()
                return f
            for n, (a, b) in enumerate(pairs):
                steps.append(mkpair(n, a, b))

            def mkgrank(sh):
                def f(v):
                    v.tensor_tensor(g3(p3_rg)[:, :, 0:4 - sh], g3(p3_gs)[:, :, sh:4],
                                    g3(p3_gs)[:, :, 0:4 - sh], ALU.is_gt)
                    v.tensor_tensor(g3(p3_rg)[:, :, 4 - sh:4], g3(p3_gs)[:, :, 0:sh],
                                    g3(p3_gs)[:, :, 4 - sh:4], ALU.is_ge)
                    v.drain()
                    if sh == 1:
                        v.tensor_copy(p3_gm[:, :], p3_rg[:, :])
                    else:
                        v.tensor_tensor(p3_gm[:, :], p3_gm[:, :], p3_rg[:, :], ALU.add)
                    v.drain()
                return f
            for sh in range(1, 4):
                steps.append(mkgrank(sh))

            def step_mask(v):
                v.tensor_scalar(p3_gm[:, :], p3_gm[:, :], 2.0, None, ALU.is_lt)
                v.drain()
                for j in range(4):
                    v.tensor_copy(s4(p3_em)[:, :, :, j], g3(p3_gm))
                v.drain()
                v.tensor_tensor(p3_msk[:, :], p3_spb[:, :], p3_em[:, :], ALU.mult)
                v.tensor_scalar(p3_tmp[:, :], p3_em[:, :], -1.0, 1.0, ALU.mult, ALU.add)
                v.drain()
                v.tensor_tensor(p3_msk[:, :], p3_msk[:, :], p3_tmp[:, :], ALU.subtract)
                v.drain()
            steps.append(step_mask)

            def mkr16(sh0):
                def f(v):
                    for sh in (sh0, sh0 + 1):
                        if sh > 15:
                            break
                        v.tensor_tensor(s3(p3_tmp)[:, :, 0:16 - sh], s3(p3_msk)[:, :, sh:16],
                                        s3(p3_msk)[:, :, 0:16 - sh], ALU.is_gt)
                        v.tensor_tensor(s3(p3_tmp)[:, :, 16 - sh:16], s3(p3_msk)[:, :, 0:sh],
                                        s3(p3_msk)[:, :, 16 - sh:16], ALU.is_ge)
                        v.drain()
                        if sh == 1:
                            v.tensor_copy(p3_r16[:, :], p3_tmp[:, :])
                        else:
                            v.tensor_tensor(p3_r16[:, :], p3_r16[:, :], p3_tmp[:, :], ALU.add)
                        v.drain()
                return f
            for sh0 in range(1, 16, 2):
                steps.append(mkr16(sh0))

            def step_tw(v):
                v.tensor_scalar(p3_r16[:, :], p3_r16[:, :], 4.0, None, ALU.is_lt)
                v.drain()
                v.tensor_tensor(p3_tw[:, :], scores[:, :], p3_r16[:, :], ALU.mult)
                v.drain()
                v.tensor_reduce(p3_gs[:, 0:NT], s3(p3_tw), axis=AX.X, op=ALU.add)
                v.drain()
                v.tensor_scalar(p3_gs[:, 0:NT], p3_gs[:, 0:NT], 1e-20, None, ALU.add)
                v.drain()
                v.reciprocal(p3_rg[:, 0:NT], p3_gs[:, 0:NT])
                v.drain()
                v.tensor_scalar(p3_rg[:, 0:NT], p3_rg[:, 0:NT], 2.5, None, ALU.mult)
                v.drain()
            steps.append(step_tw)

            def step_comb(v):
                last = None
                for e in range(16):
                    last = v.tensor_tensor(s3(combine)[:, :, e], s3(p3_tw)[:, :, e],
                                           p3_rg[:, 0:NT], ALU.mult)
                srt.inc(last, 1)
            steps.append(step_comb)
            return steps

        rsteps = build_routing_steps()

        with nc.Block() as block:
            def p4c_v1(v):
                stt.wait(v)
                for hs in range(2):
                    for hp in range(8):
                        v.reduce_max(smx[hs * 64:(hs + 1) * 64, hp:hp + 1], zap(hs, hp), axis=AX.X)
                v.drain()
                sv.inc(v.tensor_scalar(smx[:, 8:16], smx[:, 0:8], -0.125, None, ALU.mult), 1)
            block.vector(p4c_v1)

            def p4c_s(sc):
                sv.wait(sc)
                a = None
                for hs in range(2):
                    for hp in range(8):
                        a = sc.activation(eap(hs, hp), zap(hs, hp), AF.Exp,
                                          bias=smx[hs * 64:(hs + 1) * 64, 8 + hp:9 + hp],
                                          scale=f32(0.125))
                ssc.inc(a, 1)
            block.scalar(p4c_s)
            SM_EXP = ssc.v

            def p4c_v2(v):
                ssc.wait(v)
                for hs in range(2):
                    for hp in range(8):
                        v.reduce_sum(smx[hs * 64:(hs + 1) * 64, 16 + hp:17 + hp], eap(hs, hp), axis=AX.X)
                v.drain()
                v.reciprocal(smx[:, 24:32], smx[:, 16:24])
                v.drain()
                a = None
                for hs in range(2):
                    for hp in range(8):
                        c = hp * 128 + hs * 64
                        a = v.tensor_scalar(adiag[hs * 64:(hs + 1) * 64, c:c + 64], eap(hs, hp),
                                            smx[hs * 64:(hs + 1) * 64, 24 + hp:25 + hp], None, ALU.mult)
                sv.inc(a, 1)
            block.vector(p4c_v2)
            SM_DONE = sv.v

            for sp in range(NSPAN):
                for hp in range(8):
                    def p4e_v(t, sp=sp, hp=hp):
                        if sp == 0 and hp == 0:
                            swt.wait_at(t, WO_SWT[0])
                        else:
                            sev.wait_at(t, sev.v - 1)
                        mm = None
                        for c in range(8):
                            mm = t.matmul(bankf(hp % 2, 512),
                                          wT3[:, c * D + hp * 128: c * D + (hp + 1) * 128],
                                          xnT[:, c * T + sp * 512: c * T + (sp + 1) * 512],
                                          start=(c == 0), stop=(c == 7))
                        stt.inc(mm, 1)
                    block.tensor(p4e_v)

                    def p4e_ve(v, sp=sp, hp=hp):
                        stt.wait(v)
                        sev.inc(v.tensor_copy(vspan[:, hp * 512:(hp + 1) * 512], bankf(hp % 2, 512)), 1)
                    block.vector(p4e_ve)

                for hp in range(8):
                    def p4e_a(t, sp=sp, hp=hp):
                        if sp == 0 and hp == 0:
                            sv.wait_at(t, SM_DONE)
                        sev.wait_at(t, sev.v - 1)
                        stt.inc(t.matmul(bankf(2 + hp % 2, 512), adiag[:, hp * 128:(hp + 1) * 128],
                                         vspan[:, hp * 512:(hp + 1) * 512],
                                         start=True, stop=True), 1)
                    block.tensor(p4e_a)

                    def p4e_ae(v, sp=sp, hp=hp):
                        stt.wait(v)
                        sev.inc(v.tensor_copy(aspan[:, hp * 512:(hp + 1) * 512], bankf(2 + hp % 2, 512)), 1)
                    block.vector(p4e_ae)

                for rsx in range(4 * sp, 4 * sp + 4):
                    if rsx < len(rsteps):
                        block.vector(rsteps[rsx])

                for tt in range(4):
                    gidx = sp * 4 + tt
                    par = gidx % 2
                    saop = sao0 if par == 0 else sao1
                    sxep = sxe0 if par == 0 else sxe1
                    ob = 4 + 2 * (gidx % 2)

                    def p4e_xl(s, gidx=gidx, par=par, sxep=sxep):
                        if gidx >= 2:
                            sv.wait_at(s, sv.v - 1)
                        sxep.inc(s.dma_start(xeb[:, par * D:(par + 1) * D],
                                             x[gidx * P:(gidx + 1) * P, :]), 16)
                    block.sync(p4e_xl)

                    def p4e_o(t, sp=sp, tt=tt, ob=ob):
                        sev.wait(t)
                        if sp == 0 and tt == 0:
                            ssc.wait_at(t, SM_EXP)
                        if sv.v > 1:
                            sv.wait_at(t, sv.v - 1)
                        mm = None
                        for hp in range(8):
                            lhs = aspan[:, hp * 512 + tt * 128: hp * 512 + (tt + 1) * 128]
                            for half in range(2):
                                mm = t.matmul(bankf(ob + half, 512), lhs,
                                              wT4[:, hp * D + half * 512: hp * D + half * 512 + 512],
                                              start=(hp == 0), stop=(hp == 7))
                        stt.inc(mm, 1)
                    block.tensor(p4e_o)

                    def p4e_oe(v, par=par, saop=saop, sxep=sxep, ob=ob):
                        stt.wait(v)
                        sxep.wait(v)
                        saop.wait(v)
                        v.tensor_tensor(otile[:, par * D: par * D + 512], bankf(ob, 512),
                                        xeb[:, par * D: par * D + 512], ALU.add)
                        sv.inc(v.tensor_tensor(otile[:, par * D + 512: par * D + 1024],
                                               bankf(ob + 1, 512),
                                               xeb[:, par * D + 512: par * D + 1024], ALU.add), 1)
                    block.vector(p4e_oe)

                    def p4e_st(s, gidx=gidx, par=par, saop=saop):
                        sv.wait(s)
                        saop.inc(s.dma_start(out[gidx * P:(gidx + 1) * P, :],
                                             otile[:, par * D:(par + 1) * D]), 16)
                    block.sync(p4e_st)

            for rs in rsteps[4 * NSPAN:]:
                block.vector(rs)
            ROUTING_DONE = srt.v

            if debug:
                def p4e_dbg(s):
                    sao0.wait(s)
                    sao1.wait(s)
                    scst.inc(s.dma_start(dbg["attnO"], out), 16)
                block.sync(p4e_dbg)

        if debug:
            with nc.Block() as block:
                def pdbg(s):
                    srt.wait_at(s, ROUTING_DONE)
                    scst.inc(s.dma_start(dbg["scores"], scores[:, :]), 16)
                    scst.inc(s.dma_start(dbg["combine"], combine[:, :]), 16)
                block.sync(pdbg)

                def pdbg_g(g):
                    sev.wait(g)
                    sw.inc(g.dma_start(dbg["xnT"], xnT[:, :]), 16)
                block.gpsimd(pdbg_g)

        with nc.Block() as block:
            def w_aps(kind, e):
                if kind == "r":
                    return (gatew[e].rearrange("(c p) f -> p c f", p=P),
                            upw[e].rearrange("(c p) f -> p c f", p=P),
                            downw[e].rearrange("(c p) f -> p c f", p=P))
                return (shg[:, e * FF:(e + 1) * FF].rearrange("(c p) f -> p c f", p=P),
                        shu[:, e * FF:(e + 1) * FF].rearrange("(c p) f -> p c f", p=P),
                        shd[e * FF:(e + 1) * FF, :].rearrange("(c p) f -> p c f", p=P))

            wg_at = {}; wu_at = {}; wd_at = {}
            wg_atA = {}; wu_atA = {}

            def p5_w0(g):
                stt.wait(g)
                srt.wait_at(g, ROUTING_DONE)
                sao0.wait(g); sao1.wait(g)
                sxe0.wait(g); sxe1.wait(g)
                ga, ua, da = w_aps(*jobs[0][1:])
                swg.inc(g.dma_start(gwb[:, :].rearrange("p (c f) -> p c f", f=FF), ga), 16)
                swu.inc(g.dma_start(uwb[:, :].rearrange("p (c f) -> p c f", f=FF), ua), 16)
                swd.inc(g.dma_start(dwb[:, :].rearrange("p (c f) -> p c f", f=D), da), 16)
            block.gpsimd(p5_w0)
            wg_at[0] = swg.v; wu_at[0] = swu.v; wd_at[0] = swd.v

            MS_AT = {}
            ACC_AT = {}

            for ji, (h, kind, e) in enumerate(jobs):
                first_of_half = (ji == 0) or (jobs[ji - 1][0] != h)
                if first_of_half:
                    for msp in range(SPH):
                        def p5_ms(v, h=h, msp=msp):
                            if h == 1:
                                saccs[msp].wait_at(v, ACC_AT[(0, msp)])
                            sv.inc(v.memset(acc[:, msp * 4 * D:(msp + 1) * 4 * D], 0.0), 1)
                        block.vector(p5_ms)
                        MS_AT[(h, msp)] = sv.v

                if SPH % 2 == 0:
                    spairs = [(2 * k, 2 * k + 1) for k in range(SPH // 2)]
                else:
                    spairs = [(s, None) for s in range(SPH)]
                last_of_half = (ji + 1 == len(jobs)) or (jobs[ji + 1][0] != h)

                for pi, (sa, sb) in enumerate(spairs):
                    last_pair = (pi == len(spairs) - 1)
                    for ft in range(8):
                        def p5_g(t, h=h, sa=sa, sb=sb, ft=ft, ji=ji,
                                 first=(pi == 0 and ft == 0)):
                            if first:
                                if ji > 0:
                                    swgA.wait_at(t, wg_atA[ji])
                                else:
                                    swg.wait_at(t, wg_at[ji])
                            if pi == 0 and ft == 4 and ji > 0:
                                swg.wait_at(t, wg_at[ji])
                            ssi.wait(t)
                            mm = None
                            for c in range(8):
                                lhs = gwb[:, c * FF + ft * 128: c * FF + (ft + 1) * 128]
                                mm = t.matmul(bankf(0, 512), lhs,
                                              xnT[:, c * T + h * HALF + sa * 512: c * T + h * HALF + (sa + 1) * 512],
                                              start=(c == 0), stop=(c == 7))
                                if sb is not None:
                                    mm = t.matmul(bankf(6, 512), lhs,
                                                  xnT[:, c * T + h * HALF + sb * 512: c * T + h * HALF + (sb + 1) * 512],
                                                  start=(c == 0), stop=(c == 7))
                            stt.inc(mm, 1)
                        block.tensor(p5_g)

                        def p5_si(sc, sb=sb):
                            stt.wait(sc)
                            sgp.wait(sc)
                            a = sc.activation(gfb[:, :], bankf(0, 512), AF.Silu,
                                              bias=f32(0.0), scale=f32(1.0))
                            if sb is not None:
                                a = sc.activation(gfb2[:, :], bankf(6, 512), AF.Silu,
                                                  bias=f32(0.0), scale=f32(1.0))
                            ssi.inc(a, 1)
                        block.scalar(p5_si)

                        def p5_u(t, h=h, sa=sa, sb=sb, ft=ft, ji=ji,
                                 first=(pi == 0 and ft == 0)):
                            if first:
                                if ji > 0:
                                    swuA.wait_at(t, wu_atA[ji])
                                else:
                                    swu.wait_at(t, wu_at[ji])
                            if pi == 0 and ft == 4 and ji > 0:
                                swu.wait_at(t, wu_at[ji])
                            sgp.wait(t)
                            mm = None
                            for c in range(8):
                                lhs = uwb[:, c * FF + ft * 128: c * FF + (ft + 1) * 128]
                                mm = t.matmul(bankf(1, 512), lhs,
                                              xnT[:, c * T + h * HALF + sa * 512: c * T + h * HALF + (sa + 1) * 512],
                                              start=(c == 0), stop=(c == 7))
                                if sb is not None:
                                    mm = t.matmul(bankf(7, 512), lhs,
                                                  xnT[:, c * T + h * HALF + sb * 512: c * T + h * HALF + (sb + 1) * 512],
                                                  start=(c == 0), stop=(c == 7))
                            stt.inc(mm, 1)
                        block.tensor(p5_u)

                        def p5_gp(v, ft=ft, sb=sb):
                            stt.wait(v)
                            ssi.wait(v)
                            a = v.tensor_tensor(gpf[:, ft * 512:(ft + 1) * 512],
                                                bankf(1, 512), gfb[:, :], ALU.mult)
                            if sb is not None:
                                a = v.tensor_tensor(gpf2[:, ft * 512:(ft + 1) * 512],
                                                    bankf(7, 512), gfb2[:, :], ALU.mult)
                            sgp.inc(a, 1)
                        block.vector(p5_gp)

                        if last_pair and ft == 3 and ji + 1 < len(jobs):
                            def p5_pgA(g, ji=ji, at=stt.v):
                                stt.wait_at(g, at)
                                ga, _, _ = w_aps(*jobs[ji + 1][1:])
                                swgA.inc(g.dma_start(
                                    gwb[:, :].rearrange("p (c f) -> p c f", f=FF)[:, :, 0:512],
                                    ga[:, :, 0:512]), 16)
                            block.gpsimd(p5_pgA)
                            wg_atA[ji + 1] = swgA.v

                            def p5_puA(g, ji=ji, at=stt.v):
                                stt.wait_at(g, at)
                                _, ua, _ = w_aps(*jobs[ji + 1][1:])
                                swuA.inc(g.dma_start(
                                    uwb[:, :].rearrange("p (c f) -> p c f", f=FF)[:, :, 0:512],
                                    ua[:, :, 0:512]), 16)
                            block.gpsimd(p5_puA)
                            wu_atA[ji + 1] = swuA.v
                        if last_pair and ft == 7 and ji + 1 < len(jobs):
                            def p5_pg(g, ji=ji, at=stt.v):
                                stt.wait_at(g, at)
                                ga, _, _ = w_aps(*jobs[ji + 1][1:])
                                swg.inc(g.dma_start(
                                    gwb[:, :].rearrange("p (c f) -> p c f", f=FF)[:, :, 512:1024],
                                    ga[:, :, 512:1024]), 16)
                            block.gpsimd(p5_pg)
                            wg_at[ji + 1] = swg.v

                            def p5_pu(g, ji=ji, at=stt.v):
                                stt.wait_at(g, at)
                                _, ua, _ = w_aps(*jobs[ji + 1][1:])
                                swu.inc(g.dma_start(
                                    uwb[:, :].rearrange("p (c f) -> p c f", f=FF)[:, :, 512:1024],
                                    ua[:, :, 512:1024]), 16)
                            block.gpsimd(p5_pu)
                            wu_at[ji + 1] = swu.v

                    for sp, gpb in ((sa, gpf), (sb, gpf2)):
                        if sp is None:
                            continue
                        for tt in range(4):
                            bpar = 2 if tt % 2 == 0 else 4

                            def p5_d(t, h=h, sp=sp, tt=tt, ji=ji, gpb=gpb, bpar=bpar,
                                     first=(pi == 0 and sp == sa and tt == 0)):
                                if first:
                                    swd.wait_at(t, wd_at[ji])
                                sgp.wait(t)
                                if sde.v - 2 > 0:
                                    sde.wait_at(t, sde.v - 2)
                                mm = None
                                for ft in range(8):
                                    lhs = gpb[:, ft * 512 + tt * 128: ft * 512 + (tt + 1) * 128]
                                    mm = t.matmul(bankf(bpar, 512), lhs,
                                                  dwb[:, ft * D: ft * D + 512],
                                                  start=(ft == 0), stop=(ft == 7))
                                    mm = t.matmul(bankf(bpar + 1, 512), lhs,
                                                  dwb[:, ft * D + 512: ft * D + 1024],
                                                  start=(ft == 0), stop=(ft == 7))
                                stt.inc(mm, 1)
                            block.tensor(p5_d)

                            def p5_de(v, h=h, sp=sp, tt=tt, kind=kind, e=e, ji=ji, bpar=bpar,
                                      msat=MS_AT[(h, sp)],
                                      first=(first_of_half and tt == 0)):
                                stt.wait(v)
                                if first:
                                    sv.wait_at(v, msat)
                                if ji == 0 and sp == 0 and tt == 0:
                                    srt.wait_at(v, ROUTING_DONE)
                                if kind == "r":
                                    gcol = 16 * (h * NTH + sp * 4 + tt) + e
                                    scal = combine[:, gcol:gcol + 1]
                                else:
                                    scal = 1.0
                                base = (sp * 4 + tt) * D
                                d0 = acc[:, base: base + 512]
                                sde.inc(v.scalar_tensor_tensor(d0, bankf(bpar, 512), scal, d0,
                                                               ALU.mult, ALU.add), 1)
                                d1 = acc[:, base + 512: base + 1024]
                                sde.inc(v.scalar_tensor_tensor(d1, bankf(bpar + 1, 512), scal, d1,
                                                               ALU.mult, ALU.add), 1)
                            block.vector(p5_de)

                        if last_of_half:
                            def p5_acc(g, h=h, sp=sp):
                                sde.wait(g)
                                sao0.wait(g); sao1.wait(g)
                                if debug:
                                    scst.wait(g)
                                r0 = h * HALF + sp * 512
                                saccs[sp].inc(g.dma_start(
                                    out[r0:r0 + 512, :].rearrange("(tl p) d -> p tl d", p=P),
                                    acc[:, sp * 4 * D:(sp + 1) * 4 * D].rearrange("p (tl d) -> p tl d", d=D),
                                    accum_op=ALU.add), 16)
                            block.gpsimd(p5_acc)
                            ACC_AT[(h, sp)] = saccs[sp].v

                if ji + 1 < len(jobs):
                    def p5_pd(g, ji=ji, at=stt.v):
                        stt.wait_at(g, at)
                        _, _, da = w_aps(*jobs[ji + 1][1:])
                        swd.inc(g.dma_start(
                            dwb[:, :].rearrange("p (c f) -> p c f", f=D), da), 16)
                    block.gpsimd(p5_pd)
                    wd_at[ji + 1] = swd.v

        with nc.Block() as block:
            def fin(s):
                for sx in saccs:
                    sx.wait(s)
                if debug:
                    scst.wait(s)
            block.sync(fin)

    return nc


_NC_CACHE = {}


def _get_nc(key=("full",)):
    if key not in _NC_CACHE:
        _NC_CACHE[key] = build_nc()
    return _NC_CACHE[key]


def kernel(L0_attn_x, L0_attn_Wq, L0_attn_Wk, L0_attn_Wv, L0_attn_Wo,
           L0_ffn_router_w, L0_ffn_bias, L0_ffn_gate_w, L0_ffn_up_w,
           L0_ffn_down_w, L0_ffn_sh_gate_w, L0_ffn_sh_up_w, L0_ffn_sh_down_w):
    xs = np.ascontiguousarray(np.asarray(L0_attn_x, dtype=np.float32))
    B, S, Dm = xs.shape
    assert (B, S, Dm) == (8, 4096, 1024)
    com = {
        "Wq": np.ascontiguousarray(np.asarray(L0_attn_Wq, np.float32)),
        "Wk": np.ascontiguousarray(np.asarray(L0_attn_Wk, np.float32)),
        "Wv": np.ascontiguousarray(np.asarray(L0_attn_Wv, np.float32)),
        "Wo": np.ascontiguousarray(np.asarray(L0_attn_Wo, np.float32)),
        "rw": np.ascontiguousarray(np.asarray(L0_ffn_router_w, np.float32)),
        "rbias": np.ascontiguousarray(np.asarray(L0_ffn_bias, np.float32)).reshape(1, NEXP),
        "gatew": np.ascontiguousarray(np.asarray(L0_ffn_gate_w, np.float32)),
        "upw": np.ascontiguousarray(np.asarray(L0_ffn_up_w, np.float32)),
        "downw": np.ascontiguousarray(np.asarray(L0_ffn_down_w, np.float32)),
        "shg": np.ascontiguousarray(np.asarray(L0_ffn_sh_gate_w, np.float32)),
        "shu": np.ascontiguousarray(np.asarray(L0_ffn_sh_up_w, np.float32)),
        "shd": np.ascontiguousarray(np.asarray(L0_ffn_sh_down_w, np.float32)),
    }
    nc = _get_nc()
    in_maps = [dict(com, x=xs[b]) for b in range(8)]
    res = run_bass_kernel_spmd(nc, in_maps, core_ids=list(range(8)))
    return np.stack([res.results[b]["out"] for b in range(8)], axis=0)


# version 23
# speedup vs baseline: 1.1883x; 1.1883x over previous
import contextlib
import numpy as np
import ml_dtypes
import concourse.bass as bass
import concourse.mybir as mybir
from concourse.bass_utils import run_bass_kernel_spmd

F32 = mybir.dt.float32
BF16 = mybir.dt.bfloat16
AF = mybir.ActivationFunctionType
ALU = mybir.AluOpType
AX = mybir.AxisListType
f32 = np.float32

P = 128
D = 1024
DC = 8
NEXP = 16
FF = 1024
EPS = 1e-5


class Sem:
    def __init__(self, ctx, nc, name):
        self.h = ctx.enter_context(nc.semaphore(name))
        self.v = 0

    def inc(self, instr, n=1):
        instr.then_inc(self.h, n)
        self.v += n
        return instr

    def wait(self, eng):
        eng.wait_ge(self.h, self.v)

    def wait_at(self, eng, v):
        eng.wait_ge(self.h, v)


def build_nc(T=4096, routed=tuple(range(NEXP)), debug=False):
    NT = T // P
    NSPAN = T // 512
    HALF = T // 2
    NTH = NT // 2
    SPH = HALF // 512
    assert T % 1024 == 0
    experts = [("r", e) for e in routed] + [("s", 0), ("s", 1)]
    jobs = [(h, kind, e) for h in range(2) for (kind, e) in experts]

    nc = bass.Bass("TRN2", target_bir_lowering=False)

    x = nc.dram_tensor("x", [T, D], F32, kind="ExternalInput").ap()
    Wq = nc.dram_tensor("Wq", [D, D], F32, kind="ExternalInput").ap()
    Wk = nc.dram_tensor("Wk", [D, D], F32, kind="ExternalInput").ap()
    Wv = nc.dram_tensor("Wv", [D, D], F32, kind="ExternalInput").ap()
    Wo = nc.dram_tensor("Wo", [D, D], F32, kind="ExternalInput").ap()
    rw = nc.dram_tensor("rw", [NEXP, D], F32, kind="ExternalInput").ap()
    rbias = nc.dram_tensor("rbias", [1, NEXP], F32, kind="ExternalInput").ap()
    gatew = nc.dram_tensor("gatew", [NEXP, D, FF], F32, kind="ExternalInput").ap()
    upw = nc.dram_tensor("upw", [NEXP, D, FF], F32, kind="ExternalInput").ap()
    downw = nc.dram_tensor("downw", [NEXP, FF, D], F32, kind="ExternalInput").ap()
    shg = nc.dram_tensor("shg", [D, 2 * FF], F32, kind="ExternalInput").ap()
    shu = nc.dram_tensor("shu", [D, 2 * FF], F32, kind="ExternalInput").ap()
    shd = nc.dram_tensor("shd", [2 * FF, D], F32, kind="ExternalInput").ap()

    ident = nc.inline_tensor(np.eye(P, dtype=ml_dtypes.bfloat16), name="ident").ap()
    ones1 = nc.inline_tensor(np.ones((1, P), dtype=np.float32), name="ones1").ap()

    out = nc.dram_tensor("out", [T, D], F32, kind="ExternalOutput").ap()

    dbg = {}
    if debug:
        for nm, shp in [("xnT", [P, DC * T]), ("scores", [P, NT * 16]),
                        ("combine", [P, NT * 16]), ("attnO", [T, D])]:
            dbg[nm] = nc.dram_tensor("dbg_" + nm, shp, F32, kind="ExternalOutput").ap()

    ctx = contextlib.ExitStack()
    with ctx:
        HEAP = 206 * 1024
        heap = ctx.enter_context(nc.sbuf_tensor("heap", [P, HEAP], mybir.dt.uint8))

        def cf(off, n):
            assert off % 64 == 0 and off + 4 * n <= HEAP, (off, n)
            return heap[:, off:off + 4 * n].bitcast(F32)

        def cb(off, n):
            assert off % 64 == 0 and off + 2 * n <= HEAP, (off, n)
            return heap[:, off:off + 2 * n].bitcast(BF16)

        def align(v):
            return (v + 63) & ~63

        o = 0
        xnT = cb(o, DC * T); o += 2 * DC * T
        scores = cf(o, NT * 16); o += 4 * NT * 16
        combine = cf(o, NT * 16); o += 4 * NT * 16
        st = cf(o, 6 * NT); o += align(4 * 6 * NT)
        ST_MS, ST_SQ, ST_M, ST_R, ST_A, ST_B = (i * NT for i in range(6))
        smx = cf(o, 32); o += 128
        biasbc = cf(o, 16); o += 64
        rwT = cb(o, 128); o += 256
        idn = cb(o, 128); o += 256
        rbias_sb = cf(o, 16); o += 64
        ones_sb = cf(o, 128); o += 512
        adiag = cb(o, 8 * 128); o += 2 * 8 * 128
        rwst = cb(o, 1024); o += 2048
        o = align(o)
        LONG_END = o

        o = LONG_END
        xbuf = cf(o, 2 * D); o += 8 * D
        xnbf = cb(o, 2 * D); o += 4 * D
        bst = cf(o, 16); o += 64
        escore = cf(o, 16); o += 64
        P12_END = o
        assert P12_END <= LONG_END + 24 * 1024
        o = LONG_END
        p3_spb = cf(o, NT * 16); o += 4 * NT * 16
        p3_msk = cf(o, NT * 16); o += 4 * NT * 16
        p3_r16 = cf(o, NT * 16); o += 4 * NT * 16
        p3_tmp = cf(o, NT * 16); o += 4 * NT * 16
        p3_em = cf(o, NT * 16); o += 4 * NT * 16
        p3_tw = cf(o, NT * 16); o += 4 * NT * 16
        p3_gs = cf(o, NT * 4); o += align(4 * NT * 4)
        p3_rg = cf(o, NT * 4); o += align(4 * NT * 4)
        p3_gm = cf(o, NT * 4); o += align(4 * NT * 4)
        assert o <= LONG_END + 16 * 1024 + 4096, o

        P4O = LONG_END + 24 * 1024
        wst = cb(P4O, DC * D)
        wT1 = cb(P4O + 16 * 1024, DC * D)
        wT2 = cb(P4O + 32 * 1024, DC * D)
        wT3 = cb(P4O + 48 * 1024, DC * D)
        wT4 = cb(P4O + 64 * 1024, DC * D)
        qt = cb(P4O + 80 * 1024, D)
        kt = cb(P4O + 82 * 1024, D)
        P4_END = P4O + 84 * 1024
        esc = cf(P4O + 32 * 1024, 1024)
        vspan = cb(P4O, 8 * 512)
        aspan = cb(P4O + 8 * 1024, 8 * 512)
        otile = cf(P4O + 16 * 1024, 2 * D)
        xeb = cf(P4O + 24 * 1024, 2 * D)
        lgst = cf(HEAP - 2048, NT * 16)
        assert NT * 16 * 4 <= 2048

        o = LONG_END
        gwb = cb(o, DC * FF); o += 2 * DC * FF
        uwb = cb(o, DC * FF); o += 2 * DC * FF
        dwb = cb(o, DC * D); o += 2 * DC * D
        gfb = cb(o, 512); o += 2 * 512
        gfb2 = cb(o, 512); o += 2 * 512
        gpf = cb(o, 8 * 512); o += 2 * 8 * 512
        gpf2 = cb(o, 8 * 512); o += 2 * 8 * 512
        o = align(o)
        acc = cf(o, NTH * D); o += 4 * NTH * D
        P5_END = o

        assert max(P12_END, P4_END, P5_END) <= HEAP - 2048, (P12_END, P4_END, P5_END, HEAP)

        pheap = ctx.enter_context(nc.psum_tensor("pheap", [P, 8 * 512], F32))

        def bankf(b, n=512, col=0):
            return pheap[:, b * 512 + col: b * 512 + col + n]

        def bankb(b, n, col=0):
            return pheap[:, b * 512:(b + 1) * 512].bitcast(BF16)[:, col:col + n]

        def zap(hs, hp):
            c = hp * 128 + hs * 64
            return pheap[:, 4 * 512 + c: 4 * 512 + c + 64][hs * 64:(hs + 1) * 64, :]

        def eap(hs, hp):
            c = hp * 128 + hs * 64
            return esc[hs * 64:(hs + 1) * 64, c:c + 64]

        sx0 = Sem(ctx, nc, "sx0"); sx1 = Sem(ctx, nc, "sx1")
        sw = Sem(ctx, nc, "sw")
        sv = Sem(ctx, nc, "sv")
        ssc = Sem(ctx, nc, "ssc")
        stt = Sem(ctx, nc, "stt")
        sev = Sem(ctx, nc, "sev")
        ssig = Sem(ctx, nc, "ssig")
        sao0 = Sem(ctx, nc, "sao0"); sao1 = Sem(ctx, nc, "sao1")
        scst = Sem(ctx, nc, "scst")
        swg = Sem(ctx, nc, "swg"); swu = Sem(ctx, nc, "swu"); swd = Sem(ctx, nc, "swd")
        swgA = Sem(ctx, nc, "swgA"); swuA = Sem(ctx, nc, "swuA")
        ssi = Sem(ctx, nc, "ssi"); sgp = Sem(ctx, nc, "sgp"); sde = Sem(ctx, nc, "sde")
        sxe0 = Sem(ctx, nc, "sxe0"); sxe1 = Sem(ctx, nc, "sxe1")
        s2t = Sem(ctx, nc, "s2t"); s2e = Sem(ctx, nc, "s2e")
        swt = Sem(ctx, nc, "swt")
        srt = Sem(ctx, nc, "srt")
        sxt0 = Sem(ctx, nc, "sxt0"); sxt1 = Sem(ctx, nc, "sxt1")
        saccs = [Sem(ctx, nc, "sacc%d" % k) for k in range(max(1, 4))]

        with nc.Block() as block:
            def p0_sync(s):
                scst.inc(s.dma_start(idn, ident), 16)
                scst.inc(s.dma_start(biasbc[:, :], rbias.broadcast_to((P, NEXP))), 16)
            block.sync(p0_sync)

            def p0_gp(g):
                sw.inc(g.dma_start(rwst[0:16, :], rw), 16)
                sw.inc(g.dma_start(
                    wst[:, :].rearrange("p (c f) -> p c f", f=D),
                    Wq.rearrange("(c p) f -> p c f", p=P)), 16)
                sw.inc(g.dma_start(
                    wT3[:, :].rearrange("p (c f) -> p c f", f=D),
                    Wk.rearrange("(c p) f -> p c f", p=P)), 16)
            block.gpsimd(p0_gp)

        def emit_wtrans(block, wdst, sw_need):
            base = sev.v
            for rc in range(8):
                for dcc in range(8):
                    n = rc * 8 + dcc

                    def tb(t, rc=rc, dcc=dcc, n=n, base=base, sw_need=sw_need):
                        if n == 0:
                            sw.wait_at(t, sw_need)
                            scst.wait(t)
                        if sev.v - 1 >= base:
                            sev.wait_at(t, sev.v - 1)
                        elif base > 0:
                            sev.wait_at(t, base)
                        stt.inc(t.transpose(bankb(n % 2, 128),
                                            wst[:, rc * D + dcc * 128: rc * D + (dcc + 1) * 128],
                                            idn), 1)
                    block.tensor(tb)

                    def sb(sc, rc=rc, dcc=dcc, n=n):
                        stt.wait(sc)
                        sev.inc(sc.copy(wdst[:, dcc * D + rc * 128: dcc * D + (rc + 1) * 128],
                                        bankb(n % 2, 128)), 1)
                    block.scalar(sb)

        def wtile_T(wdst, rc):
            return wdst[:, :].rearrange("p (c t) -> p c t", t=D)[:, :, rc * 128:(rc + 1) * 128]

        var_at = [0] * NT; rstd_at = [0] * NT
        ts_at = [0] * NT; sxe_at = [0] * NT; score_at = [0] * NT; rt_at = [0] * NT

        with nc.Block() as block:
            def p12_ms(v):
                v.memset(adiag[:, :], 0.0)
                sv.inc(v.memset(pheap[:, 4 * 512:6 * 512], 0.0), 1)
            block.vector(p12_ms)
            Z_MS = sv.v
            WO_SWT = [0]

            def p12_wq_t(s):
                sw.wait_at(s, 48)
                scst.wait(s)
                swt.inc(s.dma_start(
                    rwT[:, :].rearrange("p (c j) -> p c j", j=16),
                    rwst[0:16, :], transpose=True), 16)
                for rc in range(8):
                    swt.inc(s.dma_start(wtile_T(wT1, rc), wst[:, rc * D:(rc + 1) * D],
                                        transpose=True), 16)
                for rc in range(8):
                    swt.inc(s.dma_start(wtile_T(wT2, rc), wT3[:, rc * D:(rc + 1) * D],
                                        transpose=True), 16)
            block.sync(p12_wq_t)
            WQ_SWT = swt.v - 128
            WK_SWT_CONST = WQ_SWT + 128

            LAG = min(10, NT - 1)

            def emit_qk(k):
                sxtp_k = sxt0 if k % 2 == 0 else sxt1

                def p4b_t(t, k=k, sxtp_k=sxtp_k):
                    sxtp_k.wait_at(t, rt_at[k])
                    if k == 0:
                        sv.wait_at(t, Z_MS)
                        swt.wait_at(t, WK_SWT_CONST)
                    else:
                        sev.wait(t)
                        for hp in range(8):
                            b = 4 + hp // 4
                            colc = (hp % 4) * 128
                            t.matmul(bankf(b, 128, colc), kt[:, hp * 128:(hp + 1) * 128],
                                     qt[:, hp * 128:(hp + 1) * 128],
                                     start=False, stop=False, skip_group_check=True)
                    mm = None
                    for c in range(8):
                        lhs = xnT[:, c * T + k * P: c * T + (k + 1) * P]
                        for wT, b0, b1 in ((wT1, 0, 1), (wT2, 2, 6)):
                            mm = t.matmul(bankf(b0, 512), lhs,
                                          wT[:, c * D: c * D + 512],
                                          start=(c == 0), stop=(c == 7))
                            mm = t.matmul(bankf(b1, 512), lhs,
                                          wT[:, c * D + 512: c * D + 1024],
                                          start=(c == 0), stop=(c == 7))
                    stt.inc(mm, 1)
                block.tensor(p4b_t)

                def p4b_v(v, k=k):
                    stt.wait(v)
                    v.tensor_copy(qt[:, 0:512], bankf(0, 512))
                    v.tensor_copy(qt[:, 512:1024], bankf(1, 512))
                    v.tensor_copy(kt[:, 0:512], bankf(2, 512))
                    sev.inc(v.tensor_copy(kt[:, 512:1024], bankf(6, 512)), 1)
                block.vector(p4b_v)


            def emit_router(i):
                rb = 3 if i % 2 == 0 else 7
                sxtp_i = sxt0 if i % 2 == 0 else sxt1

                def p12_mm(t, i=i, rb=rb, sxtp_i=sxtp_i):
                    sxtp_i.wait_at(t, rt_at[i])
                    if i == 0:
                        swt.wait_at(t, WK_SWT_CONST)
                    if i >= 2:
                        ssig.wait_at(t, ssig.v - 1)
                    mm = None
                    for c in range(8):
                        mm = t.matmul(bankf(rb, 16), xnT[:, c * T + i * P: c * T + (i + 1) * P],
                                      rwT[:, c * 16:(c + 1) * 16],
                                      start=(c == 0), stop=(c == 7))
                    stt.inc(mm, 1)
                block.tensor(p12_mm)

                def p12_lc(v, i=i, rb=rb):
                    stt.wait(v)
                    ssig.inc(v.tensor_copy(lgst[:, 16 * i:16 * i + 16], bankf(rb, 16)), 1)
                block.vector(p12_lc)


            for i in range(NT):
                par = i % 2
                sxp = sx0 if par == 0 else sx1
                sxtp = sxt0 if par == 0 else sxt1

                if i >= LAG:
                    emit_qk(i - LAG)
                if NT >= 16 and i > NT - 8:
                    _q = NT // 8
                    for rr in range(_q * (i - (NT - 8) - 1), _q * (i - (NT - 8) - 1) + _q):
                        emit_router(rr)

                def p12_ld(s, i=i, par=par, sxp=sxp):
                    if i >= 2:
                        sv.wait_at(s, ts_at[i - 2])
                    sxp.inc(s.dma_start(xbuf[:, par * D:(par + 1) * D],
                                        x[i * P:(i + 1) * P, :]), 16)
                block.sync(p12_ld)

                def p12_v1(v, i=i, par=par, sxp=sxp):
                    sxp.wait(v)
                    if i > 0:
                        v.drain()
                    xb = xbuf[:, par * D:(par + 1) * D]
                    v.bn_stats(bst[:, 0:6], xb[:, 0:512])
                    v.bn_stats(bst[:, 6:12], xb[:, 512:1024])
                    v.drain()
                    sv.inc(v.bn_aggr(st[:, 2 * i:2 * i + 2], bst[:, 0:12]), 1)
                block.vector(p12_v1)
                var_at[i] = sv.v

                def p12_s2(sc, i=i):
                    sv.wait_at(sc, var_at[i])
                    sc.activation(st[:, ST_B + i:ST_B + i + 1], st[:, 2 * i + 1:2 * i + 2],
                                  AF.Ln, bias=f32(EPS), scale=f32(1.0))
                    sc.drain()
                    ssc.inc(sc.activation(st[:, ST_R + i:ST_R + i + 1], st[:, ST_B + i:ST_B + i + 1],
                                          AF.Exp, bias=f32(0.0), scale=f32(-0.5)), 1)
                block.scalar(p12_s2)
                rstd_at[i] = ssc.v

                def p12_v2(v, i=i, par=par, sxtp=sxtp):
                    ssc.wait_at(v, rstd_at[i])
                    if i >= 2:
                        sxtp.wait(v)
                    sv.inc(v.tensor_scalar(
                        xnbf[:, par * D:(par + 1) * D], xbuf[:, par * D:(par + 1) * D],
                        st[:, 2 * i:2 * i + 1], st[:, ST_R + i:ST_R + i + 1],
                        ALU.subtract, ALU.mult), 1)
                block.vector(p12_v2)
                ts_at[i] = sv.v

                def p12_xt(s, i=i, par=par, sxtp=sxtp):
                    sv.wait_at(s, ts_at[i])
                    sxtp.inc(s.dma_start(
                        xnT[:, :].rearrange("p (c t) -> p c t", t=T)[:, :, i * 128:(i + 1) * 128],
                        xnbf[:, par * D:(par + 1) * D], transpose=True), 16)
                block.sync(p12_xt)
                rt_at[i] = sxtp.v

                if i == NT // 2:
                    def p4b_wv_t(s):
                        sw.wait_at(s, 64)
                        for rc in range(8):
                            swt.inc(s.dma_start(wtile_T(wT3, rc), wst[:, rc * D:(rc + 1) * D],
                                                transpose=True), 16)
                    block.sync(p4b_wv_t)
                    WV_SWT = swt.v

                    def p4b_stage_wo(g):
                        swt.wait_at(g, WV_SWT)
                        sw.wait_at(g, 64)
                        sw.inc(g.dma_start(
                            wst[:, :].rearrange("p (c f) -> p c f", f=D),
                            Wo.rearrange("(c p) f -> p c f", p=P)), 16)
                    block.gpsimd(p4b_stage_wo)
                    WO_SW = sw.v

                if i == 3 * NT // 4:
                    def p4b_wo_t(s):
                        sw.wait_at(s, WO_SW)
                        for rc in range(8):
                            swt.inc(s.dma_start(wtile_T(wT4, rc), wst[:, rc * D:(rc + 1) * D],
                                                transpose=True), 16)
                    block.sync(p4b_wo_t)
                    WO_SWT[0] = swt.v

                if i == max(1, NT // 8):
                    def p12_stage_wv(g):
                        swt.wait_at(g, WK_SWT_CONST)
                        sw.wait_at(g, 48)
                        sw.inc(g.dma_start(
                            wst[:, :].rearrange("p (c f) -> p c f", f=D),
                            Wv.rearrange("(c p) f -> p c f", p=P)), 16)
                    block.gpsimd(p12_stage_wv)

            for k in range(NT - LAG, NT):
                emit_qk(k)

            if NT < 16:
                for i in range(NT):
                    emit_router(i)
            else:
                for i in range(NT - NT // 8, NT):
                    emit_router(i)

            def p12_sexp(sc):
                ssig.wait(sc)
                ssc.inc(sc.activation(p3_tmp[:, 0:NT * 16], lgst[:, 0:NT * 16],
                                      AF.Exp, bias=f32(0.0), scale=f32(-1.0)), 1)
            block.scalar(p12_sexp)

            def p12_srec(v):
                ssc.wait(v)
                v.tensor_scalar(p3_r16[:, :], p3_tmp[:, 0:NT * 16], 1.0, None, ALU.add)
                v.drain()
                sv.inc(v.reciprocal(scores[:, :], p3_r16[:, :]), 1)
            block.vector(p12_srec)
            SCORES_BATCH = sv.v

            def p4b_fin(t):
                sev.wait(t)
                mm = None
                for hp in range(8):
                    b = 4 + hp // 4
                    colc = (hp % 4) * 128
                    mm = t.matmul(bankf(b, 128, colc), kt[:, hp * 128:(hp + 1) * 128],
                                  qt[:, hp * 128:(hp + 1) * 128],
                                  start=False, stop=False, skip_group_check=True)
                stt.inc(mm, 1)
            block.tensor(p4b_fin)

        SCORES_DONE = SCORES_BATCH

        def build_routing_steps():
            s3 = lambda ap: ap.rearrange("p (t e) -> p t e", e=16)
            s4 = lambda ap: ap.rearrange("p (t g j) -> p t g j", g=4, j=4)
            g3 = lambda ap: ap.rearrange("p (t g) -> p t g", g=4)
            steps = []

            def step0(v):
                sv.wait_at(v, SCORES_DONE)
                scst.wait(v)
                for tix in range(NT):
                    v.tensor_tensor(p3_spb[:, 16 * tix:16 * tix + 16],
                                    scores[:, 16 * tix:16 * tix + 16], biasbc[:, :], ALU.add)
                v.drain()
            steps.append(step0)

            pairs = [(a, b) for a in range(4) for b in range(a + 1, 4)]

            def mkpair(n, a, b):
                def f(v):
                    dst = p3_gs if n == 0 else p3_rg
                    v.tensor_tensor(g3(dst), s4(p3_spb)[:, :, :, a], s4(p3_spb)[:, :, :, b], ALU.add)
                    v.drain()
                    if n > 0:
                        v.tensor_tensor(p3_gs[:, :], p3_gs[:, :], p3_rg[:, :], ALU.max)
                        v.drain()
                return f
            for n, (a, b) in enumerate(pairs):
                steps.append(mkpair(n, a, b))

            def mkgrank(sh):
                def f(v):
                    v.tensor_tensor(g3(p3_rg)[:, :, 0:4 - sh], g3(p3_gs)[:, :, sh:4],
                                    g3(p3_gs)[:, :, 0:4 - sh], ALU.is_gt)
                    v.tensor_tensor(g3(p3_rg)[:, :, 4 - sh:4], g3(p3_gs)[:, :, 0:sh],
                                    g3(p3_gs)[:, :, 4 - sh:4], ALU.is_ge)
                    v.drain()
                    if sh == 1:
                        v.tensor_copy(p3_gm[:, :], p3_rg[:, :])
                    else:
                        v.tensor_tensor(p3_gm[:, :], p3_gm[:, :], p3_rg[:, :], ALU.add)
                    v.drain()
                return f
            for sh in range(1, 4):
                steps.append(mkgrank(sh))

            def step_mask(v):
                v.tensor_scalar(p3_gm[:, :], p3_gm[:, :], 2.0, None, ALU.is_lt)
                v.drain()
                for j in range(4):
                    v.tensor_copy(s4(p3_em)[:, :, :, j], g3(p3_gm))
                v.drain()
                v.tensor_tensor(p3_msk[:, :], p3_spb[:, :], p3_em[:, :], ALU.mult)
                v.tensor_scalar(p3_tmp[:, :], p3_em[:, :], -1.0, 1.0, ALU.mult, ALU.add)
                v.drain()
                v.tensor_tensor(p3_msk[:, :], p3_msk[:, :], p3_tmp[:, :], ALU.subtract)
                v.drain()
            steps.append(step_mask)

            def mkr16(sh0):
                def f(v):
                    for sh in (sh0, sh0 + 1):
                        if sh > 15:
                            break
                        v.tensor_tensor(s3(p3_tmp)[:, :, 0:16 - sh], s3(p3_msk)[:, :, sh:16],
                                        s3(p3_msk)[:, :, 0:16 - sh], ALU.is_gt)
                        v.tensor_tensor(s3(p3_tmp)[:, :, 16 - sh:16], s3(p3_msk)[:, :, 0:sh],
                                        s3(p3_msk)[:, :, 16 - sh:16], ALU.is_ge)
                        v.drain()
                        if sh == 1:
                            v.tensor_copy(p3_r16[:, :], p3_tmp[:, :])
                        else:
                            v.tensor_tensor(p3_r16[:, :], p3_r16[:, :], p3_tmp[:, :], ALU.add)
                        v.drain()
                return f
            for sh0 in range(1, 16, 2):
                steps.append(mkr16(sh0))

            def step_tw(v):
                v.tensor_scalar(p3_r16[:, :], p3_r16[:, :], 4.0, None, ALU.is_lt)
                v.drain()
                v.tensor_tensor(p3_tw[:, :], scores[:, :], p3_r16[:, :], ALU.mult)
                v.drain()
                v.tensor_reduce(p3_gs[:, 0:NT], s3(p3_tw), axis=AX.X, op=ALU.add)
                v.drain()
                v.tensor_scalar(p3_gs[:, 0:NT], p3_gs[:, 0:NT], 1e-20, None, ALU.add)
                v.drain()
                v.reciprocal(p3_rg[:, 0:NT], p3_gs[:, 0:NT])
                v.drain()
                v.tensor_scalar(p3_rg[:, 0:NT], p3_rg[:, 0:NT], 2.5, None, ALU.mult)
                v.drain()
            steps.append(step_tw)

            def step_comb(v):
                last = None
                for e in range(16):
                    last = v.tensor_tensor(s3(combine)[:, :, e], s3(p3_tw)[:, :, e],
                                           p3_rg[:, 0:NT], ALU.mult)
                srt.inc(last, 1)
            steps.append(step_comb)
            return steps

        rsteps = build_routing_steps()

        with nc.Block() as block:
            def p4c_v1(v):
                stt.wait(v)
                for hs in range(2):
                    for hp in range(8):
                        v.reduce_max(smx[hs * 64:(hs + 1) * 64, hp:hp + 1], zap(hs, hp), axis=AX.X)
                v.drain()
                sv.inc(v.tensor_scalar(smx[:, 8:16], smx[:, 0:8], -0.125, None, ALU.mult), 1)
            block.vector(p4c_v1)

            def p4c_s(sc):
                sv.wait(sc)
                a = None
                for hs in range(2):
                    for hp in range(8):
                        a = sc.activation(eap(hs, hp), zap(hs, hp), AF.Exp,
                                          bias=smx[hs * 64:(hs + 1) * 64, 8 + hp:9 + hp],
                                          scale=f32(0.125))
                ssc.inc(a, 1)
            block.scalar(p4c_s)
            SM_EXP = ssc.v

            def p4c_v2(v):
                ssc.wait(v)
                for hs in range(2):
                    for hp in range(8):
                        v.reduce_sum(smx[hs * 64:(hs + 1) * 64, 16 + hp:17 + hp], eap(hs, hp), axis=AX.X)
                v.drain()
                v.reciprocal(smx[:, 24:32], smx[:, 16:24])
                v.drain()
                a = None
                for hs in range(2):
                    for hp in range(8):
                        c = hp * 128 + hs * 64
                        a = v.tensor_scalar(adiag[hs * 64:(hs + 1) * 64, c:c + 64], eap(hs, hp),
                                            smx[hs * 64:(hs + 1) * 64, 24 + hp:25 + hp], None, ALU.mult)
                sv.inc(a, 1)
            block.vector(p4c_v2)
            SM_DONE = sv.v

            for sp in range(NSPAN):
                for hp in range(8):
                    def p4e_v(t, sp=sp, hp=hp):
                        if sp == 0 and hp == 0:
                            swt.wait_at(t, WO_SWT[0])
                        else:
                            sev.wait_at(t, sev.v - 1)
                        mm = None
                        for c in range(8):
                            mm = t.matmul(bankf(hp % 2, 512),
                                          wT3[:, c * D + hp * 128: c * D + (hp + 1) * 128],
                                          xnT[:, c * T + sp * 512: c * T + (sp + 1) * 512],
                                          start=(c == 0), stop=(c == 7))
                        stt.inc(mm, 1)
                    block.tensor(p4e_v)

                    def p4e_ve(v, sp=sp, hp=hp):
                        stt.wait(v)
                        sev.inc(v.tensor_copy(vspan[:, hp * 512:(hp + 1) * 512], bankf(hp % 2, 512)), 1)
                    block.vector(p4e_ve)

                for hp in range(8):
                    def p4e_a(t, sp=sp, hp=hp):
                        if sp == 0 and hp == 0:
                            sv.wait_at(t, SM_DONE)
                        sev.wait_at(t, sev.v - 1)
                        stt.inc(t.matmul(bankf(2 + hp % 2, 512), adiag[:, hp * 128:(hp + 1) * 128],
                                         vspan[:, hp * 512:(hp + 1) * 512],
                                         start=True, stop=True), 1)
                    block.tensor(p4e_a)

                    def p4e_ae(v, sp=sp, hp=hp):
                        stt.wait(v)
                        sev.inc(v.tensor_copy(aspan[:, hp * 512:(hp + 1) * 512], bankf(2 + hp % 2, 512)), 1)
                    block.vector(p4e_ae)

                for rsx in range(4 * sp, 4 * sp + 4):
                    if rsx < len(rsteps):
                        block.vector(rsteps[rsx])

                for tt in range(4):
                    gidx = sp * 4 + tt
                    par = gidx % 2
                    saop = sao0 if par == 0 else sao1
                    sxep = sxe0 if par == 0 else sxe1
                    ob = 4 + 2 * (gidx % 2)

                    def p4e_xl(s, gidx=gidx, par=par, sxep=sxep):
                        if gidx >= 2:
                            sv.wait_at(s, sv.v - 1)
                        sxep.inc(s.dma_start(xeb[:, par * D:(par + 1) * D],
                                             x[gidx * P:(gidx + 1) * P, :]), 16)
                    block.sync(p4e_xl)

                    def p4e_o(t, sp=sp, tt=tt, ob=ob):
                        sev.wait(t)
                        if sp == 0 and tt == 0:
                            ssc.wait_at(t, SM_EXP)
                        if sv.v > 1:
                            sv.wait_at(t, sv.v - 1)
                        mm = None
                        for hp in range(8):
                            lhs = aspan[:, hp * 512 + tt * 128: hp * 512 + (tt + 1) * 128]
                            for half in range(2):
                                mm = t.matmul(bankf(ob + half, 512), lhs,
                                              wT4[:, hp * D + half * 512: hp * D + half * 512 + 512],
                                              start=(hp == 0), stop=(hp == 7))
                        stt.inc(mm, 1)
                    block.tensor(p4e_o)

                    def p4e_oe(v, par=par, saop=saop, sxep=sxep, ob=ob):
                        stt.wait(v)
                        sxep.wait(v)
                        saop.wait(v)
                        v.tensor_tensor(otile[:, par * D: par * D + 512], bankf(ob, 512),
                                        xeb[:, par * D: par * D + 512], ALU.add)
                        sv.inc(v.tensor_tensor(otile[:, par * D + 512: par * D + 1024],
                                               bankf(ob + 1, 512),
                                               xeb[:, par * D + 512: par * D + 1024], ALU.add), 1)
                    block.vector(p4e_oe)

                    def p4e_st(s, gidx=gidx, par=par, saop=saop):
                        sv.wait(s)
                        saop.inc(s.dma_start(out[gidx * P:(gidx + 1) * P, :],
                                             otile[:, par * D:(par + 1) * D]), 16)
                    block.sync(p4e_st)

            for rs in rsteps[4 * NSPAN:]:
                block.vector(rs)
            ROUTING_DONE = srt.v

            if debug:
                def p4e_dbg(s):
                    sao0.wait(s)
                    sao1.wait(s)
                    scst.inc(s.dma_start(dbg["attnO"], out), 16)
                block.sync(p4e_dbg)

        if debug:
            with nc.Block() as block:
                def pdbg(s):
                    srt.wait_at(s, ROUTING_DONE)
                    scst.inc(s.dma_start(dbg["scores"], scores[:, :]), 16)
                    scst.inc(s.dma_start(dbg["combine"], combine[:, :]), 16)
                block.sync(pdbg)

                def pdbg_g(g):
                    sev.wait(g)
                    sw.inc(g.dma_start(dbg["xnT"], xnT[:, :]), 16)
                block.gpsimd(pdbg_g)

        with nc.Block() as block:
            def w_aps(kind, e):
                if kind == "r":
                    return (gatew[e].rearrange("(c p) f -> p c f", p=P),
                            upw[e].rearrange("(c p) f -> p c f", p=P),
                            downw[e].rearrange("(c p) f -> p c f", p=P))
                return (shg[:, e * FF:(e + 1) * FF].rearrange("(c p) f -> p c f", p=P),
                        shu[:, e * FF:(e + 1) * FF].rearrange("(c p) f -> p c f", p=P),
                        shd[e * FF:(e + 1) * FF, :].rearrange("(c p) f -> p c f", p=P))

            wg_at = {}; wu_at = {}; wd_at = {}
            wg_atA = {}; wu_atA = {}

            def p5_w0(g):
                stt.wait(g)
                srt.wait_at(g, ROUTING_DONE)
                sao0.wait(g); sao1.wait(g)
                sxe0.wait(g); sxe1.wait(g)
                ga, ua, da = w_aps(*jobs[0][1:])
                swg.inc(g.dma_start(gwb[:, :].rearrange("p (c f) -> p c f", f=FF), ga), 16)
                swu.inc(g.dma_start(uwb[:, :].rearrange("p (c f) -> p c f", f=FF), ua), 16)
                swd.inc(g.dma_start(dwb[:, :].rearrange("p (c f) -> p c f", f=D), da), 16)
            block.gpsimd(p5_w0)
            wg_at[0] = swg.v; wu_at[0] = swu.v; wd_at[0] = swd.v

            MS_AT = {}
            ACC_AT = {}

            for ji, (h, kind, e) in enumerate(jobs):
                first_of_half = (ji == 0) or (jobs[ji - 1][0] != h)
                if first_of_half:
                    for msp in range(SPH):
                        def p5_ms(v, h=h, msp=msp):
                            if h == 1:
                                saccs[msp].wait_at(v, ACC_AT[(0, msp)])
                            sv.inc(v.memset(acc[:, msp * 4 * D:(msp + 1) * 4 * D], 0.0), 1)
                        block.vector(p5_ms)
                        MS_AT[(h, msp)] = sv.v

                if SPH % 2 == 0:
                    spairs = [(2 * k, 2 * k + 1) for k in range(SPH // 2)]
                else:
                    spairs = [(s, None) for s in range(SPH)]
                last_of_half = (ji + 1 == len(jobs)) or (jobs[ji + 1][0] != h)

                for pi, (sa, sb) in enumerate(spairs):
                    last_pair = (pi == len(spairs) - 1)
                    for ft in range(8):
                        def p5_g(t, h=h, sa=sa, sb=sb, ft=ft, ji=ji,
                                 first=(pi == 0 and ft == 0)):
                            if first:
                                if ji > 0:
                                    swgA.wait_at(t, wg_atA[ji])
                                else:
                                    swg.wait_at(t, wg_at[ji])
                            if pi == 0 and ft == 4 and ji > 0:
                                swg.wait_at(t, wg_at[ji])
                            ssi.wait(t)
                            mm = None
                            for c in range(8):
                                lhs = gwb[:, c * FF + ft * 128: c * FF + (ft + 1) * 128]
                                mm = t.matmul(bankf(0, 512), lhs,
                                              xnT[:, c * T + h * HALF + sa * 512: c * T + h * HALF + (sa + 1) * 512],
                                              start=(c == 0), stop=(c == 7))
                                if sb is not None:
                                    mm = t.matmul(bankf(6, 512), lhs,
                                                  xnT[:, c * T + h * HALF + sb * 512: c * T + h * HALF + (sb + 1) * 512],
                                                  start=(c == 0), stop=(c == 7))
                            stt.inc(mm, 1)
                        block.tensor(p5_g)

                        def p5_si(sc, sb=sb):
                            stt.wait(sc)
                            sgp.wait(sc)
                            a = sc.activation(gfb[:, :], bankf(0, 512), AF.Silu,
                                              bias=f32(0.0), scale=f32(1.0))
                            if sb is not None:
                                a = sc.activation(gfb2[:, :], bankf(6, 512), AF.Silu,
                                                  bias=f32(0.0), scale=f32(1.0))
                            ssi.inc(a, 1)
                        block.scalar(p5_si)

                        def p5_u(t, h=h, sa=sa, sb=sb, ft=ft, ji=ji,
                                 first=(pi == 0 and ft == 0)):
                            if first:
                                if ji > 0:
                                    swuA.wait_at(t, wu_atA[ji])
                                else:
                                    swu.wait_at(t, wu_at[ji])
                            if pi == 0 and ft == 4 and ji > 0:
                                swu.wait_at(t, wu_at[ji])
                            sgp.wait(t)
                            mm = None
                            for c in range(8):
                                lhs = uwb[:, c * FF + ft * 128: c * FF + (ft + 1) * 128]
                                mm = t.matmul(bankf(1, 512), lhs,
                                              xnT[:, c * T + h * HALF + sa * 512: c * T + h * HALF + (sa + 1) * 512],
                                              start=(c == 0), stop=(c == 7))
                                if sb is not None:
                                    mm = t.matmul(bankf(7, 512), lhs,
                                                  xnT[:, c * T + h * HALF + sb * 512: c * T + h * HALF + (sb + 1) * 512],
                                                  start=(c == 0), stop=(c == 7))
                            stt.inc(mm, 1)
                        block.tensor(p5_u)

                        def p5_gp(v, ft=ft, sb=sb):
                            stt.wait(v)
                            ssi.wait(v)
                            a = v.tensor_tensor(gpf[:, ft * 512:(ft + 1) * 512],
                                                bankf(1, 512), gfb[:, :], ALU.mult)
                            if sb is not None:
                                a = v.tensor_tensor(gpf2[:, ft * 512:(ft + 1) * 512],
                                                    bankf(7, 512), gfb2[:, :], ALU.mult)
                            sgp.inc(a, 1)
                        block.vector(p5_gp)

                        if last_pair and ft == 3 and ji + 1 < len(jobs):
                            def p5_pgA(g, ji=ji, at=stt.v):
                                stt.wait_at(g, at)
                                ga, _, _ = w_aps(*jobs[ji + 1][1:])
                                swgA.inc(g.dma_start(
                                    gwb[:, :].rearrange("p (c f) -> p c f", f=FF)[:, :, 0:512],
                                    ga[:, :, 0:512]), 16)
                            block.gpsimd(p5_pgA)
                            wg_atA[ji + 1] = swgA.v

                            def p5_puA(g, ji=ji, at=stt.v):
                                stt.wait_at(g, at)
                                _, ua, _ = w_aps(*jobs[ji + 1][1:])
                                swuA.inc(g.dma_start(
                                    uwb[:, :].rearrange("p (c f) -> p c f", f=FF)[:, :, 0:512],
                                    ua[:, :, 0:512]), 16)
                            block.gpsimd(p5_puA)
                            wu_atA[ji + 1] = swuA.v
                        if last_pair and ft == 7 and ji + 1 < len(jobs):
                            def p5_pg(g, ji=ji, at=stt.v):
                                stt.wait_at(g, at)
                                ga, _, _ = w_aps(*jobs[ji + 1][1:])
                                swg.inc(g.dma_start(
                                    gwb[:, :].rearrange("p (c f) -> p c f", f=FF)[:, :, 512:1024],
                                    ga[:, :, 512:1024]), 16)
                            block.gpsimd(p5_pg)
                            wg_at[ji + 1] = swg.v

                            def p5_pu(g, ji=ji, at=stt.v):
                                stt.wait_at(g, at)
                                _, ua, _ = w_aps(*jobs[ji + 1][1:])
                                swu.inc(g.dma_start(
                                    uwb[:, :].rearrange("p (c f) -> p c f", f=FF)[:, :, 512:1024],
                                    ua[:, :, 512:1024]), 16)
                            block.gpsimd(p5_pu)
                            wu_at[ji + 1] = swu.v

                    for sp, gpb in ((sa, gpf), (sb, gpf2)):
                        if sp is None:
                            continue
                        for tt in range(4):
                            bpar = 2 if tt % 2 == 0 else 4

                            def p5_d(t, h=h, sp=sp, tt=tt, ji=ji, gpb=gpb, bpar=bpar,
                                     first=(pi == 0 and sp == sa and tt == 0)):
                                if first:
                                    swd.wait_at(t, wd_at[ji])
                                sgp.wait(t)
                                if sde.v - 2 > 0:
                                    sde.wait_at(t, sde.v - 2)
                                mm = None
                                for ft in range(8):
                                    lhs = gpb[:, ft * 512 + tt * 128: ft * 512 + (tt + 1) * 128]
                                    mm = t.matmul(bankf(bpar, 512), lhs,
                                                  dwb[:, ft * D: ft * D + 512],
                                                  start=(ft == 0), stop=(ft == 7))
                                    mm = t.matmul(bankf(bpar + 1, 512), lhs,
                                                  dwb[:, ft * D + 512: ft * D + 1024],
                                                  start=(ft == 0), stop=(ft == 7))
                                stt.inc(mm, 1)
                            block.tensor(p5_d)

                            def p5_de(v, h=h, sp=sp, tt=tt, kind=kind, e=e, ji=ji, bpar=bpar,
                                      msat=MS_AT[(h, sp)],
                                      first=(first_of_half and tt == 0)):
                                stt.wait(v)
                                if first:
                                    sv.wait_at(v, msat)
                                if ji == 0 and sp == 0 and tt == 0:
                                    srt.wait_at(v, ROUTING_DONE)
                                if kind == "r":
                                    gcol = 16 * (h * NTH + sp * 4 + tt) + e
                                    scal = combine[:, gcol:gcol + 1]
                                else:
                                    scal = 1.0
                                base = (sp * 4 + tt) * D
                                d0 = acc[:, base: base + 512]
                                sde.inc(v.scalar_tensor_tensor(d0, bankf(bpar, 512), scal, d0,
                                                               ALU.mult, ALU.add), 1)
                                d1 = acc[:, base + 512: base + 1024]
                                sde.inc(v.scalar_tensor_tensor(d1, bankf(bpar + 1, 512), scal, d1,
                                                               ALU.mult, ALU.add), 1)
                            block.vector(p5_de)

                        if last_of_half:
                            def p5_acc(g, h=h, sp=sp):
                                sde.wait(g)
                                sao0.wait(g); sao1.wait(g)
                                if debug:
                                    scst.wait(g)
                                r0 = h * HALF + sp * 512
                                saccs[sp].inc(g.dma_start(
                                    out[r0:r0 + 512, :].rearrange("(tl p) d -> p tl d", p=P),
                                    acc[:, sp * 4 * D:(sp + 1) * 4 * D].rearrange("p (tl d) -> p tl d", d=D),
                                    accum_op=ALU.add), 16)
                            block.gpsimd(p5_acc)
                            ACC_AT[(h, sp)] = saccs[sp].v

                if ji + 1 < len(jobs):
                    def p5_pd(g, ji=ji, at=stt.v):
                        stt.wait_at(g, at)
                        _, _, da = w_aps(*jobs[ji + 1][1:])
                        swd.inc(g.dma_start(
                            dwb[:, :].rearrange("p (c f) -> p c f", f=D), da), 16)
                    block.gpsimd(p5_pd)
                    wd_at[ji + 1] = swd.v

        with nc.Block() as block:
            def fin(s):
                for sx in saccs:
                    sx.wait(s)
                if debug:
                    scst.wait(s)
            block.sync(fin)

    return nc


_NC_CACHE = {}


def _get_nc(key=("full",)):
    if key not in _NC_CACHE:
        _NC_CACHE[key] = build_nc()
    return _NC_CACHE[key]


def kernel(L0_attn_x, L0_attn_Wq, L0_attn_Wk, L0_attn_Wv, L0_attn_Wo,
           L0_ffn_router_w, L0_ffn_bias, L0_ffn_gate_w, L0_ffn_up_w,
           L0_ffn_down_w, L0_ffn_sh_gate_w, L0_ffn_sh_up_w, L0_ffn_sh_down_w):
    xs = np.ascontiguousarray(np.asarray(L0_attn_x, dtype=np.float32))
    B, S, Dm = xs.shape
    assert (B, S, Dm) == (8, 4096, 1024)
    com = {
        "Wq": np.ascontiguousarray(np.asarray(L0_attn_Wq, np.float32)),
        "Wk": np.ascontiguousarray(np.asarray(L0_attn_Wk, np.float32)),
        "Wv": np.ascontiguousarray(np.asarray(L0_attn_Wv, np.float32)),
        "Wo": np.ascontiguousarray(np.asarray(L0_attn_Wo, np.float32)),
        "rw": np.ascontiguousarray(np.asarray(L0_ffn_router_w, np.float32)),
        "rbias": np.ascontiguousarray(np.asarray(L0_ffn_bias, np.float32)).reshape(1, NEXP),
        "gatew": np.ascontiguousarray(np.asarray(L0_ffn_gate_w, np.float32)),
        "upw": np.ascontiguousarray(np.asarray(L0_ffn_up_w, np.float32)),
        "downw": np.ascontiguousarray(np.asarray(L0_ffn_down_w, np.float32)),
        "shg": np.ascontiguousarray(np.asarray(L0_ffn_sh_gate_w, np.float32)),
        "shu": np.ascontiguousarray(np.asarray(L0_ffn_sh_up_w, np.float32)),
        "shd": np.ascontiguousarray(np.asarray(L0_ffn_sh_down_w, np.float32)),
    }
    nc = _get_nc()
    in_maps = [dict(com, x=xs[b]) for b in range(8)]
    res = run_bass_kernel_spmd(nc, in_maps, core_ids=list(range(8)))
    return np.stack([res.results[b]["out"] for b in range(8)], axis=0)
